# version 22
# baseline (speedup 1.0000x reference)
import math
import numpy as np
import concourse.bass as bass
import concourse.mybir as mybir
from concourse.bass_utils import run_bass_kernel_spmd

F32 = mybir.dt.float32
BF16 = mybir.dt.bfloat16
I32 = mybir.dt.int32
I8 = mybir.dt.int8
AF = mybir.ActivationFunctionType
ALU = mybir.AluOpType

T = 2048
D = 1024
N = 8192
NC = 64
O = 1024
NTILE = 16
TCH = 4
NG = 4
KU = 32
VU = 16
SSU = 16

SIGN_MASK = int(np.uint32(0x80000000).view(np.int32))
EXP_MASK = 0x7F800000
FP8_MAX = 448.0
MAGIC_MULT = float(1.5 * 2.0 ** 20)
EMIN = float(2.0 ** -6)
EXP_BIAS_C = float(0.5 * math.log(8192.0))


def _gelcov(zi):
    g, j = divmod(zi, 128)
    return 129 * g + 2 * (j // 2 + 1)


def _gelval(zi):
    return 129 * (zi // 128) + (zi % 128) + 1


def build_program(debug=False):
    nc = bass.Bass("TRN2", target_bir_lowering=False)

    x_hbm = nc.dram_tensor("x", [T, D], F32, kind="ExternalInput").ap()
    k_hbm = nc.dram_tensor("kk", [N, D], F32, kind="ExternalInput").ap()
    v_hbm = nc.dram_tensor("vv", [N, O], F32, kind="ExternalInput").ap()
    w_hbm = nc.dram_tensor("ww", [O, D], F32, kind="ExternalInput").ap()
    b_hbm = nc.dram_tensor("bias", [1, O], F32, kind="ExternalInput").ap()
    is_hbm = nc.dram_tensor("iscale", [1, 1], F32, kind="ExternalInput").ap()
    ws_hbm = nc.dram_tensor("wscale", [1, 1], F32, kind="ExternalInput").ap()
    out_hbm = nc.dram_tensor("out", [T, D], F32, kind="ExternalOutput").ap()
    spill = nc.dram_tensor("spill", [NC, 8, 128, 256], BF16,
                           kind="ExternalOutput" if debug else "Internal").ap()
    padscr = nc.dram_tensor("padscr", [128, 256], BF16, kind="Internal").ap()
    if debug:
        dbg_ht = nc.dram_tensor("dbg_ht", [128, 8, T], F32, kind="ExternalOutput").ap()
        dbg_scb = nc.dram_tensor("dbg_scb", [128, T], F32, kind="ExternalOutput").ap()

    import ml_dtypes
    ident_dram = nc.inline_tensor(np.eye(128, dtype=ml_dtypes.bfloat16), name="ident").ap()


    from contextlib import ExitStack
    _es = ExitStack()
    RV = _es.enter_context(nc.sbuf_tensor("RV", [128, 131072], I8))
    RA = _es.enter_context(nc.sbuf_tensor("RA", [128, 32768], I8))
    RB = _es.enter_context(nc.sbuf_tensor("RB", [128, 32768], I8))
    scB = _es.enter_context(nc.sbuf_tensor("scB", [128, 2048], F32))
    biasb = _es.enter_context(nc.sbuf_tensor("biasb", [128, 1024], F32))
    ident = _es.enter_context(nc.sbuf_tensor("identsb", [128, 128], BF16))
    onescol = _es.enter_context(nc.sbuf_tensor("onescol", [128, 1], BF16))
    epsb = _es.enter_context(nc.sbuf_tensor("epsb", [128, 1], F32))
    s_b = _es.enter_context(nc.sbuf_tensor("s_b", [128, 1], F32))
    recip_b = _es.enter_context(nc.sbuf_tensor("recip_b", [128, 1], F32))
    intc = _es.enter_context(nc.sbuf_tensor("intc", [128, 2], I32))
    cbc = _es.enter_context(nc.sbuf_tensor("cbc", [1, 4], F32))
    sums = _es.enter_context(nc.sbuf_tensor("sums", [128, 1], F32))
    sumsq = _es.enter_context(nc.sbuf_tensor("sumsq", [128, 1], F32))
    meanb = _es.enter_context(nc.sbuf_tensor("meanb", [128, 1], F32))
    varb = _es.enter_context(nc.sbuf_tensor("varb", [128, 1], F32))
    msqb = _es.enter_context(nc.sbuf_tensor("msqb", [128, 1], F32))
    nmsqb = _es.enter_context(nc.sbuf_tensor("nmsqb", [128, 1], F32))
    s0st = _es.enter_context(nc.sbuf_tensor("s0st", [128, 14], F32))
    stdb = _es.enter_context(nc.sbuf_tensor("stdb", [128, 1], F32))
    rstdb = _es.enter_context(nc.sbuf_tensor("rstdb", [128, 1], F32))
    nmrb = _es.enter_context(nc.sbuf_tensor("nmrb", [128, 1], F32))
    aapb = _es.enter_context(nc.sbuf_tensor("aapb", [128, 1], F32))
    bapb = _es.enter_context(nc.sbuf_tensor("bapb", [128, 1], F32))
    P0 = _es.enter_context(nc.psum_tensor("P0", [128, 512], F32))
    P1 = _es.enter_context(nc.psum_tensor("P1", [128, 512], F32))
    P2 = _es.enter_context(nc.psum_tensor("P2", [128, 512], F32))
    P3 = _es.enter_context(nc.psum_tensor("P3", [128, 512], F32))
    P4 = _es.enter_context(nc.psum_tensor("P4", [128, 512], F32))
    P5 = _es.enter_context(nc.psum_tensor("P5", [128, 512], F32))
    P6 = _es.enter_context(nc.psum_tensor("P6", [128, 512], F32))
    P7 = _es.enter_context(nc.psum_tensor("P7", [128, 512], F32))
    m_pro = _es.enter_context(nc.semaphore("m_pro"))
    m_prov = _es.enter_context(nc.semaphore("m_prov"))
    m_prot = _es.enter_context(nc.semaphore("m_prot"))
    m_x = _es.enter_context(nc.semaphore("m_x"))
    m_ve = _es.enter_context(nc.semaphore("m_ve"))
    m_sc = _es.enter_context(nc.semaphore("m_sc"))
    m_tt = _es.enter_context(nc.semaphore("m_tt"))
    m_ht = _es.enter_context(nc.semaphore("m_ht"))
    m_k = _es.enter_context(nc.semaphore("m_k"))
    m_kt = _es.enter_context(nc.semaphore("m_kt"))
    m_ktsb = _es.enter_context(nc.semaphore("m_ktsb"))
    m_mm1 = _es.enter_context(nc.semaphore("m_mm1"))
    m_sevac = _es.enter_context(nc.semaphore("m_sevac"))
    m_sq = _es.enter_context(nc.semaphore("m_sq"))
    m_ones = _es.enter_context(nc.semaphore("m_ones"))
    m_spill = _es.enter_context(nc.semaphore("m_spill"))
    m_v = _es.enter_context(nc.semaphore("m_v"))
    m_nr = _es.enter_context(nc.semaphore("m_nr"))
    m_nl = _es.enter_context(nc.semaphore("m_nl"))
    m_bc = _es.enter_context(nc.semaphore("m_bc"))
    m_scb = _es.enter_context(nc.semaphore("m_scb"))
    m_w = _es.enter_context(nc.semaphore("m_w"))
    m_wtt = _es.enter_context(nc.semaphore("m_wtt"))
    m_wev = _es.enter_context(nc.semaphore("m_wev"))
    m_xg = _es.enter_context(nc.semaphore("m_xg"))
    m_ssl = _es.enter_context(nc.semaphore("m_ssl"))
    m_ssl2 = _es.enter_context(nc.semaphore("m_ssl2"))
    m_z = _es.enter_context(nc.semaphore("m_z"))
    m_gel = _es.enter_context(nc.semaphore("m_gel"))
    m_mm2 = _es.enter_context(nc.semaphore("m_mm2"))
    m_bv = _es.enter_context(nc.semaphore("m_bv"))
    m_bs = _es.enter_context(nc.semaphore("m_bs"))
    m_bt = _es.enter_context(nc.semaphore("m_bt"))
    m_out = _es.enter_context(nc.semaphore("m_out"))
    spc = _es.enter_context(nc.sbuf_tensor("spc", [128, 1], F32))
    spcS = _es.enter_context(nc.sbuf_tensor("spcS", [128, 1], F32))
    block = _es.enter_context(nc.Block())
    if True:
        P = [P0, P1, P2, P3, P4, P5, P6, P7]

        def carve(region, off, p0, p1, dtype, shape2):
            nelem = 1
            for s in shape2:
                nelem *= s
            nbytes = nelem * mybir.dt.size(dtype)
            ap = region.ap()[p0:p1, off:off + nbytes].bitcast(dtype)
            if len(shape2) > 1:
                letters = "abcdefg"[: len(shape2)]
                ap = ap.rearrange(
                    "p (" + " ".join(letters) + ") -> p " + " ".join(letters),
                    **{letters[i]: shape2[i] for i in range(len(shape2))},
                )
            return ap

        V_sb = carve(RV, 0, 0, 128, BF16, [NC, 1024])
        hT = carve(RA, 0, 0, 128, BF16, [8, 2048])
        xg = carve(RA, 0, 0, 128, F32, [4, 1024])
        WT = carve(RA, 16384, 0, 128, BF16, [8, 1024])
        xstage = [carve(RB, 0, 0, 128, F32, [1024]), carve(RB, 4096, 0, 128, F32, [1024]),
                  carve(RV, 114688, 0, 128, F32, [1024]), carve(RV, 118784, 0, 128, F32, [1024])]
        dumpA = carve(RB, 8192, 0, 128, BF16, [1024])
        dumpB = carve(RB, 10240, 0, 128, BF16, [1024])
        hnorm = [carve(RB, 20480, 0, 128, BF16, [1024]), carve(RB, 22528, 0, 128, BF16, [1024])]
        bias_row = carve(RB, 12288, 0, 1, F32, [1024])
        onesr = carve(RB, 16384, 0, 1, F32, [128])
        isv = carve(RB, 16896, 0, 1, F32, [1])
        wsv = carve(RB, 16960, 0, 1, F32, [1])
        sval = carve(RB, 17024, 0, 1, F32, [1])
        kstage = [carve(RV, 122880, 0, 128, BF16, [2, 1024]), carve(RV, 126976, 0, 128, BF16, [2, 1024])]
        s_sb = [carve(RB, 12288, 0, 128, BF16, [2048]), carve(RB, 16384, 0, 128, BF16, [2048])]
        sqb = [carve(RB, 20480 + 1024 * i, 0, 128, BF16, [512]) for i in range(4)]
        KT_sb = [carve(RB, 24576, 0, 128, BF16, [1024]), carve(RB, 26624, 0, 128, BF16, [1024])]
        nrow = carve(RB, 24576, 0, 1, F32, [2048])
        onesr2 = carve(RB, 12288, 0, 1, F32, [128])
        ssl = [carve(RB, 2048 * i, 0, 128, BF16, [4, 256]) for i in range(5)]
        obuf = [carve(RB, 10240 + 2048 * i, 0, 128, F32, [512]) for i in range(2)]
        zb = [carve(RB, 14336 + 1024 * i, 0, 128, F32, [256]) for i in range(4)]
        zbig = [carve(RB, 14336, 0, 128, F32, [512]), carve(RB, 16384, 0, 128, F32, [512])]
        gbuf = [carve(RB, 18432 + 512 * i, 0, 128, BF16, [256]) for i in range(4)]
        gbig = [carve(RB, 18432, 0, 128, BF16, [512]), carve(RB, 19456, 0, 128, BF16, [512])]
        wstage = [carve(RB, 8192, 0, 128, BF16, [1024]), carve(RB, 10240, 0, 128, BF16, [1024]),
                  carve(RB, 0, 0, 128, BF16, [1024]), carve(RB, 2048, 0, 128, BF16, [1024]),
                  carve(RB, 4096, 0, 128, BF16, [1024]), carve(RB, 6144, 0, 128, BF16, [1024])]
        qT = carve(RB, 20480, 0, 128, BF16, [8, 256])
        qC = carve(RB, 24576, 0, 128, I32, [512])
        qS = carve(RB, 26624, 0, 128, I32, [512])
        qM = carve(RB, 28672, 0, 128, I32, [512])
        qCf = carve(RB, 24576, 0, 128, F32, [512])
        qSf = carve(RB, 26624, 0, 128, F32, [512])
        qMf = carve(RB, 28672, 0, 128, F32, [512])
        qb = carve(RB, 30720, 0, 128, BF16, [1024])
        dumpB2 = carve(RB, 30720, 0, 128, BF16, [1024])
        P6b = P6.ap().bitcast(BF16)
        P7b = P7.ap().bitcast(BF16)

        from collections import deque
        SC0, VE0 = {}, {}
        _c0 = 0
        for _i in range(NTILE + 2):
            if _i < NTILE:
                _c0 += 1; SC0[("acc", _i)] = _c0
            if 1 <= _i <= NTILE:
                _c0 += 1; SC0[("sqrt", _i - 1)] = _c0
            if _i >= 2:
                _c0 += 1; SC0[("hnorm", _i - 2)] = _c0
        _c0 = 0
        for _i in range(NTILE + 2):
            if _i < NTILE:
                _c0 += 1; VE0[("stats", _i)] = _c0
            if 1 <= _i <= NTILE:
                _c0 += 1; VE0[("rn", _i - 1)] = _c0
        s0 = {}
        for _k, _off in (("sums", 0), ("sumsq", 2), ("mean", 4), ("var", 6),
                         ("std", 8), ("rstd", 10), ("nmr", 12)):
            for _s in range(2):
                s0[(_k, _s)] = s0st[:, _off + _s:_off + _s + 1]

        VOPS, SOPS, TOPS = [], [], []
        BVv, BSv, BTv = {}, {}, {}
        _cnt = {"bv": 0, "bs": 0, "bt": 0}
        vq, squ, tq = deque(), deque(), deque()

        def _lazy(d, key):
            return lambda: d[key]

        def v_z(g, p, c):
            zi = 128 * g + 64 * p + c
            Pi = 2 * g + p
            def f(vec):
                vec.wait_ge(m_ssl, 16 * (17 * Pi + min(c // 4 + 3, 17)))
                if zi >= 4:
                    vec.wait_ge(m_gel, _gelcov(zi - 4))
                vec.tensor_tensor(zb[c % 4][:, :], ssl[(16 * Pi + c // 4) % 5][:, c % 4, :],
                                  scB[:, 512 * g + 256 * p:512 * g + 256 * p + 256], ALU.mult)
                vec.tensor_copy(spc[:, :], epsb[:, :]).then_inc(m_z, 1)
            VOPS.append(f)

        def v_xevac(g, p):
            _cnt["bv"] += 1
            BVv[("x2p", g, p)] = _cnt["bv"]
            def f(vec):
                vec.wait_ge(m_mm2, 128 * g + 64 * (p + 1))
                vec.wait_ge(m_xg, 32 * g + 16 * (p + 1))
                for j in range(4):
                    tb = 2 * p + j // 2
                    oh = j % 2
                    vec.tensor_add(xg[:, tb, 512 * oh:512 * (oh + 1)], P[j][:, :],
                                   xg[:, tb, 512 * oh:512 * (oh + 1)])
                vec.tensor_copy(spc[:, :], epsb[:, :])
                vec.tensor_copy(spc[:, :], epsb[:, :]).then_inc(m_bv, 1)
            VOPS.append(f)


        def vb_aap(g, tb):
            def mk():
                w = _lazy(BSv, ("std", g, tb))
                def f(vec):
                    vec.wait_ge(m_bs, w())
                    vec.reciprocal(rstdb[:, :], stdb[:, :])
                    vec.tensor_scalar(nmrb[:, :], meanb[:, :], -1.0, None, ALU.mult)
                    vec.tensor_tensor(aapb[:, :], rstdb[:, :], recip_b[:, :], ALU.mult)
                    vec.tensor_tensor(nmrb[:, :], nmrb[:, :], rstdb[:, :], ALU.mult)
                    vec.tensor_copy(spc[:, :], epsb[:, :])
                    vec.tensor_tensor(bapb[:, :], nmrb[:, :], recip_b[:, :], ALU.mult)
                    vec.tensor_copy(spc[:, :], epsb[:, :])
                VOPS.append(f)
            return mk

        def vb_norm(g, tb, h):
            def mk():
                def f(vec):
                    vec.tensor_scalar(qCf[:, 0:256], xg[:, tb, 512 * h:512 * h + 256],
                                      aapb[:, 0:1], bapb[:, 0:1], ALU.mult, ALU.add)
                    vec.tensor_scalar(qCf[:, 256:512], xg[:, tb, 512 * h + 256:512 * (h + 1)],
                                      aapb[:, 0:1], bapb[:, 0:1], ALU.mult, ALU.add)
                VOPS.append(f)
            return mk

        def vb_qbA(g, tb, h):
            def mk():
                def f(vec):
                    qsl = [slice(0, 256), slice(256, 512)]
                    for sl in qsl:
                        vec.tensor_scalar(qCf[:, sl], qCf[:, sl], -FP8_MAX, FP8_MAX,
                                          ALU.max, ALU.min)
                    for sl in qsl:
                        vec.tensor_scalar(qM[:, sl], qC[:, sl], intc[:, 1:2], None, ALU.bitwise_and)
                    for sl in qsl:
                        vec.tensor_scalar(qS[:, sl], qC[:, sl], intc[:, 0:1], None, ALU.bitwise_and)
                VOPS.append(f)
            return mk

        def vb_qbB(g, tb, h):
            def mk():
                def f(vec):
                    qsl = [slice(0, 256), slice(256, 512)]
                    for sl in qsl:
                        vec.tensor_scalar(qMf[:, sl], qMf[:, sl], EMIN, MAGIC_MULT,
                                          ALU.max, ALU.mult)
                    for sl in qsl:
                        vec.tensor_tensor(qS[:, sl], qM[:, sl], qS[:, sl], ALU.bitwise_or)
                    for sl in qsl:
                        vec.tensor_tensor(qMf[:, sl], qCf[:, sl], qSf[:, sl], ALU.add)
                VOPS.append(f)
            return mk

        def vb_qbC(g, tb, h):
            def mk():
                _cnt["bv"] += 1
                BVv[("qbsub", g, tb, h)] = _cnt["bv"]
                pq = (g, tb - 1) if tb >= 1 else ((g - 1, 3) if g >= 1 else None)
                wq = _lazy(BTv, ("qtt",) + pq) if (h == 0 and pq is not None) else None
                def f(vec):
                    if wq is not None:
                        vec.wait_ge(m_bt, wq())
                    vec.tensor_tensor(qb[:, 512 * h:512 * h + 256], qMf[:, 0:256],
                                      qSf[:, 0:256], ALU.subtract)
                    vec.tensor_tensor(qb[:, 512 * h + 256:512 * (h + 1)], qMf[:, 256:512],
                                      qSf[:, 256:512], ALU.subtract)
                    vec.tensor_copy(spc[:, :], epsb[:, :]).then_inc(m_bv, 1)
                VOPS.append(f)
            return mk

        def sb_qtev(g, tb):
            def mk():
                _cnt["bs"] += 1
                BSv[("qtev", g, tb)] = _cnt["bs"]
                w = _lazy(BTv, ("qtt", g, tb))
                def f(sca):
                    sca.wait_ge(m_bt, w())
                    sca.copy(qT[:, :, 128 * (tb % 2):128 * (tb % 2) + 128],
                             P6b[:, 0:1024].rearrange("p (a b) -> p a b", a=8, b=128))
                    sca.copy(spcS[:, :], epsb[:, :]).then_inc(m_bs, 1)
                SOPS.append(f)
            return mk

        def vb_epiA(g, tb2):
            def mk():
                w = _lazy(BTv, ("ffn", g, tb2, 1))
                def f(vec):
                    vec.wait_ge(m_bt, w())
                    for oh in range(2):
                        e = 8 * g + 2 * tb2 + oh
                        if e >= 2:
                            vec.wait_ge(m_out, 16 * (e - 1))
                        vec.tensor_add(obuf[e % 2][:, :], xg[:, tb2, 512 * oh:512 * (oh + 1)],
                                       biasb[:, 512 * oh:512 * (oh + 1)])
                VOPS.append(f)
            return mk

        def vb_epiB(g, tb2):
            def mk():
                for oh in range(2):
                    _cnt["bv"] += 1
                    BVv[("epi", g, tb2, oh)] = _cnt["bv"]
                def f(vec):
                    for oh in range(2):
                        e = 8 * g + 2 * tb2 + oh
                        vec.scalar_tensor_tensor(obuf[e % 2][:, :], P[4 + e % 2][:, :],
                                                 s_b[:, 0:1], obuf[e % 2][:, :],
                                                 ALU.mult, ALU.add)
                    vec.tensor_copy(spc[:, :], epsb[:, :]).then_inc(m_bv, 1)
                    vec.tensor_copy(spc[:, :], epsb[:, :]).then_inc(m_bv, 1)
                VOPS.append(f)
            return mk

        def s_gelu(g, p, c):
            zi = 128 * g + 64 * p + c
            def f(sca):
                sca.wait_ge(m_z, zi + 1)
                if zi >= 5:
                    sca.wait_ge(m_mm2, zi - 3)
                sca.activation(gbig[(c // 2) % 2][:, :], zbig[(c // 2) % 2][:, :], AF.Gelu)
                sca.copy(spcS[:, :], epsb[:, :]).then_inc(m_gel, 2)
            SOPS.append(f)

        def s_geltail(g):
            def f(sca):
                sca.copy(spcS[:, :], epsb[:, :]).then_inc(m_gel, 1)
            SOPS.append(f)

        def sb_sumsqA(g, tb):
            def mk():
                w = _lazy(BVv, ("x2p", g, tb // 2))
                pq = (g, tb - 1) if tb >= 1 else ((g - 1, 3) if g >= 1 else None)
                wq = _lazy(BTv, ("qtt",) + pq) if pq is not None else None
                def f(sca):
                    sca.wait_ge(m_bv, w())
                    if wq is not None:
                        sca.wait_ge(m_bt, wq())
                    sca.activation(dumpB2[:, :], xg[:, tb, :], AF.Identity, accum_out=sums[:, :])
                SOPS.append(f)
            return mk

        def sb_sumsqB(g, tb):
            def mk():
                _cnt["bs"] += 1
                BSv[("std", g, tb)] = _cnt["bs"]
                def f(sca):
                    sca.activation(dumpB2[:, :], xg[:, tb, :], AF.Square, accum_out=sumsq[:, :])
                    sca.activation(meanb[:, :], sums[:, :], AF.Identity, scale=1.0 / D)
                    sca.copy(spcS[:, :], epsb[:, :])
                    sca.activation(msqb[:, :], meanb[:, :], AF.Square)
                    sca.copy(spcS[:, :], epsb[:, :])
                    sca.activation(nmsqb[:, :], msqb[:, :], AF.Identity,
                                   bias=epsb[:, 0:1], scale=-1.0)
                    sca.copy(spcS[:, :], epsb[:, :])
                    sca.activation(stdb[:, :], sumsq[:, :], AF.Sqrt,
                                   bias=nmsqb[:, 0:1], scale=1.0 / D)
                    sca.copy(spcS[:, :], epsb[:, :]).then_inc(m_bs, 1)
                SOPS.append(f)
            return mk

        def sb_out(g, tb2):
            def mk():
                w = _lazy(BVv, ("epi", g, tb2, 1))
                def f(sca):
                    sca.wait_ge(m_bv, w())
                    for oh in range(2):
                        e = 8 * g + 2 * tb2 + oh
                        sca.dma_start(out=out_hbm[512 * g + 128 * tb2:512 * g + 128 * (tb2 + 1),
                                                  512 * oh:512 * (oh + 1)],
                                      in_=obuf[e % 2][:, :]).then_inc(m_out, 16)
                SOPS.append(f)
            return mk

        def t_mm2(g, p, c):
            zi = 128 * g + 64 * p + c
            def f(ten):
                ten.wait_ge(m_gel, min(_gelcov(zi + 1), 129 * (g + 1)))
                if p == 0 and c == 0:
                    if g == 0:
                        ten.wait_ge(m_scb, 4)
                    else:
                        ten.wait_ge(m_bv, BVv[("x2p", g - 1, 1)])
                if p == 1 and c == 0:
                    ten.wait_ge(m_bv, BVv[("x2p", g, 0)])
                if g == 0 and p == 0:
                    ten.wait_ge(m_v, 16 * (c // 4 + 1))
                for tbL in range(2):
                    for oh in range(2):
                        mm = ten.matmul(P[2 * tbL + oh][:, :],
                                        gbuf[c % 4][:, 128 * tbL:128 * (tbL + 1)],
                                        V_sb[:, c, 512 * oh:512 * (oh + 1)],
                                        start=(c == 0), stop=(c == NC - 1))
                        if tbL == 1 and oh == 1:
                            mm.then_inc(m_mm2, 1)
            TOPS.append(f)

        def tb_qtt(g, tb):
            def mk():
                _cnt["bt"] += 1
                BTv[("qtt", g, tb)] = _cnt["bt"]
                w = _lazy(BVv, ("qbsub", g, tb, 1))
                pq = (g, tb - 1) if tb >= 1 else ((g - 1, 3) if g >= 1 else None)
                wq = _lazy(BSv, ("qtev",) + pq) if pq is not None else None
                def f(ten):
                    ten.wait_ge(m_bv, w())
                    if g == 0 and tb == 0:
                        ten.wait_ge(m_wev, 8)
                    if wq is not None:
                        ten.wait_ge(m_bs, wq())
                    for ii in range(8):
                        mm = ten.transpose(P6b[:, 128 * ii:128 * (ii + 1)],
                                           qb[:, 128 * ii:128 * (ii + 1)], ident[:, :])
                        if ii == 7:
                            mm.then_inc(m_bt, 1)
                TOPS.append(f)
            return mk

        def tb_ffn(g, tb2):
            def mk():
                vals = []
                for oh in range(2):
                    _cnt["bt"] += 1
                    BTv[("ffn", g, tb2, oh)] = _cnt["bt"]
                w = _lazy(BSv, ("qtev", g, tb2))
                eps_w = []
                for oh in range(2):
                    e = 8 * g + 2 * tb2 + oh
                    if e >= 2:
                        ep = e - 2
                        gg, r = divmod(ep, 8)
                        eps_w.append(_lazy(BVv, ("epi", gg, r // 2, r % 2)))
                    else:
                        eps_w.append(None)
                def f(ten):
                    ten.wait_ge(m_bs, w())
                    if g == 0 and tb2 == 0:
                        ten.wait_ge(m_wev, 8)
                    for oh in range(2):
                        e = 8 * g + 2 * tb2 + oh
                        if eps_w[oh] is not None:
                            ten.wait_ge(m_bv, eps_w[oh]())
                        for ii in range(8):
                            mm = ten.matmul(P[4 + e % 2][:, :],
                                            qT[:, ii, 128 * (tb2 % 2):128 * (tb2 % 2) + 128],
                                            WT[:, ii, 512 * oh:512 * (oh + 1)],
                                            start=(ii == 0), stop=(ii == 7))
                            if ii == 7:
                                mm.then_inc(m_bt, 1)
                TOPS.append(f)
            return mk

        def _queue_B(g):
            vv, ss, tt = [], [], []
            for tb in range(4):
                b = 64 * (2 * g + 1 + tb // 2) + (12 + 28 * (tb % 2) if tb < 2 else 2 + 28 * (tb % 2))
                ss.append((b + 0, sb_sumsqA(g, tb)))
                ss.append((b + 1, sb_sumsqB(g, tb)))
                vv.append((b + 4, vb_aap(g, tb)))
                vv.append((b + 5, vb_norm(g, tb, 0)))
                vv.append((b + 6, vb_qbA(g, tb, 0)))
                vv.append((b + 7, vb_qbB(g, tb, 0)))
                vv.append((b + 8, vb_qbC(g, tb, 0)))
                vv.append((b + 9, vb_norm(g, tb, 1)))
                vv.append((b + 10, vb_qbA(g, tb, 1)))
                vv.append((b + 11, vb_qbB(g, tb, 1)))
                vv.append((b + 12, vb_qbC(g, tb, 1)))
                tt.append((b + 13, tb_qtt(g, tb)))
                ss.append((b + 15, sb_qtev(g, tb)))
                tt.append((b + 17, tb_ffn(g, tb)))
                vv.append((b + 26, vb_epiA(g, tb)))
                vv.append((b + 27, vb_epiB(g, tb)))
                ss.append((b + 30, sb_out(g, tb)))
            vv.sort(key=lambda x: x[0])
            ss.sort(key=lambda x: x[0])
            tt.sort(key=lambda x: x[0])
            vq.extend(vv)
            squ.extend(ss)
            tq.extend(tt)

        for g in range(NG):
            for p in range(2):
                for c in range(NC):
                    s = 64 * (2 * g + p) + c
                    v_z(g, p, c)
                    if p == 0 and c == 1 and g >= 1:
                        v_xevac(g - 1, 1)
                    while vq and vq[0][0] == s:
                        vq.popleft()[1]()
                    if p == 1 and c == 1:
                        v_xevac(g, 0)
                    if c % 2 == 1:
                        s_gelu(g, p, c)
                    while squ and squ[0][0] == s:
                        squ.popleft()[1]()
                    t_mm2(g, p, c)
                    while tq and tq[0][0] == s:
                        tq.popleft()[1]()
                if p == 0:
                    _queue_B(g)
            s_geltail(g)
        v_xevac(NG - 1, 1)
        while vq:
            vq.popleft()[1]()
        while squ:
            squ.popleft()[1]()
        while tq:
            tq.popleft()[1]()

        @block.sync
        def _(sync):
            sync.dma_start(out=bias_row, in_=b_hbm[:, :]).then_inc(m_pro, 16)
            sync.dma_start(out=isv, in_=is_hbm[:, :]).then_inc(m_pro, 16)
            sync.dma_start(out=wsv, in_=ws_hbm[:, :]).then_inc(m_pro, 16)
            sync.dma_start(out=ident[:, :], in_=ident_dram[:, :]).then_inc(m_pro, 16)
            sync.wait_ge(m_prov, 2)
            for i in range(NTILE):
                if i >= 4:
                    sync.wait_ge(m_sc, SC0[("hnorm", i - 4)])
                sync.dma_start(out=xstage[i % 4][:, :],
                               in_=x_hbm[128 * i:128 * (i + 1), :]).then_inc(m_x, 16)
            for c in range(NC):
                sync.wait_ge(m_sevac, 4 * (c + 1))
                sync.dma_start(out=spill[c, :, :, :].transpose([1, 0, 2]),
                               in_=s_sb[c % 2][:, :].rearrange("p (a b) -> p a b", a=8, b=256)
                               ).then_inc(m_spill, 16)
            for g in range(NG):
                for p in range(2):
                    if p == 1:
                        if g >= 1:
                            sync.wait_ge(m_bv, BVv[("epi", g - 1, 1, 1)])
                        sync.dma_start(out=xg[:, 0:2, :],
                                       in_=x_hbm[512 * g:512 * g + 256, :].rearrange("(b p) d -> p b d", p=128)
                                       ).then_inc(m_xg, 16)
                    Pi = 2 * g + p
                    for su in range(SSU):
                        gu = 16 * Pi + su
                        if g == 0 and p == 0 and su <= 4:
                            sync.wait_ge(m_wtt, 3 + su)
                        if gu >= 5:
                            pgu = gu - 5
                            sync.wait_ge(m_mm2, 64 * (pgu // 16) + 4 * (pgu % 16) + 4)
                        sync.dma_start(out=ssl[gu % 5][:, :, :],
                                       in_=spill[4 * su:4 * su + 4, 2 * g + p, :, :].transpose([1, 0, 2])
                                       ).then_inc(m_ssl, 16)
                    sync.dma_start(out=padscr[0:4, :], in_=spill[0, 0, 0:4, :]).then_inc(m_ssl, 16)
                    if p == 1:
                        if g >= 1:
                            sync.wait_ge(m_bv, BVv[("epi", g - 1, 3, 1)])
                        sync.dma_start(out=xg[:, 2:4, :],
                                       in_=x_hbm[512 * g + 256:512 * (g + 1), :].rearrange("(b p) d -> p b d", p=128)
                                       ).then_inc(m_xg, 16)

        @block.gpsimd
        def _(gpsimd):
            if debug:
                gpsimd.wait_ge(m_ht, NTILE)
                gpsimd.dma_start(out=dbg_ht[:, :, :], in_=hT[:, :, :]).then_inc(m_pro, 16)
            gpsimd.wait_ge(m_sc, SC0[("acc", 3)])
            for ku in range(KU):
                if ku >= 2:
                    gpsimd.wait_ge(m_kt, 2 * ku - 2)
                gpsimd.dma_start(out=kstage[ku % 2][:, :, :],
                                 in_=k_hbm[256 * ku:256 * (ku + 1), :].rearrange("(c p) d -> p c d", p=128)
                                 ).then_inc(m_k, 16)
                if ku >= 1 and (ku - 1) < VU - 2:
                    vu = ku - 1
                    gpsimd.dma_start(out=V_sb[:, 4 * vu:4 * (vu + 1), :],
                                     in_=v_hbm[512 * vu:512 * (vu + 1), :].rearrange("(c p) o -> p c o", p=128)
                                     ).then_inc(m_v, 16)
            gpsimd.wait_ge(m_sc, SC0[("acc", NTILE - 1)])
            for wr in range(8):
                if wr == 2:
                    gpsimd.wait_ge(m_sc, SC0[("hnorm", 13)])
                if wr >= 6:
                    gpsimd.wait_ge(m_wtt, wr - 5)
                gpsimd.dma_start(out=wstage[wr % 6][:, :],
                                 in_=w_hbm[128 * wr:128 * (wr + 1), :]).then_inc(m_w, 16)
            gpsimd.wait_ge(m_sc, SC0[("hnorm", NTILE - 1)])
            gpsimd.dma_start(out=V_sb[:, 56:60, :],
                             in_=v_hbm[512 * 14:512 * 15, :].rearrange("(c p) o -> p c o", p=128)
                             ).then_inc(m_v, 16)
            gpsimd.wait_ge(m_kt, NC)
            gpsimd.dma_start(out=V_sb[:, 60:64, :],
                             in_=v_hbm[512 * 15:512 * 16, :].rearrange("(c p) o -> p c o", p=128)
                             ).then_inc(m_v, 16)

            if debug:
                gpsimd.wait_ge(m_scb, 4)
                gpsimd.dma_start(out=dbg_scb[:, :], in_=scB[:, :]).then_inc(m_pro, 16)

        @block.tensor
        def _(tensor):
            tensor.wait_ge(m_pro, 64)
            tensor.wait_ge(m_prov, 1)
            tensor.matmul(P0[:, :], onesr, bias_row[:, 0:512], start=True, stop=True)
            tensor.matmul(P1[:, :], onesr, bias_row[:, 512:1024], start=True, stop=True)
            tensor.matmul(P2[:, 0:1], onesr, sval, start=True, stop=True)
            tensor.matmul(P3[:, 0:1], onesr, isv, start=True, stop=True).then_inc(m_prot, 1)
            for i in range(NTILE):
                tensor.wait_ge(m_sc, SC0[("hnorm", i)])
                if i >= 1:
                    tensor.wait_ge(m_ht, i)
                for j in range(8):
                    mm = tensor.transpose(P7b[:, 128 * j:128 * (j + 1)],
                                          hnorm[i % 2][:, 128 * j:128 * (j + 1)], ident[:, :])
                    if j == 7:
                        mm.then_inc(m_tt, 1)
            KTP = [P6b, P7b]
            def kt_transp(cc):
                tensor.wait_ge(m_k, 16 * (cc // 2 + 1))
                if cc >= 2:
                    tensor.wait_ge(m_ktsb, cc - 1)
                src = kstage[(cc // 2) % 2][:, cc % 2, :]
                for j in range(8):
                    mm = tensor.transpose(KTP[cc % 2][:, 128 * j:128 * (j + 1)],
                                          src[:, 128 * j:128 * (j + 1)], ident[:, :])
                    if j == 7:
                        mm.then_inc(m_kt, 1)
            kt_transp(0)
            for c in range(NC):
                tensor.wait_ge(m_ktsb, c + 1)
                if c == 0:
                    tensor.wait_ge(m_ht, NTILE)
                for tc in range(TCH):
                    k = 4 * c + tc
                    if k >= 2:
                        tensor.wait_ge(m_sevac, k - 1)
                    bank = P[4 + k % 2]
                    for j in range(8):
                        mm = tensor.matmul(bank[:, :], KT_sb[c % 2][:, 128 * j:128 * (j + 1)],
                                           hT[:, j, 512 * tc:512 * (tc + 1)],
                                           start=(j == 0), stop=(j == 7))
                        if j == 7:
                            mm.then_inc(m_mm1, 1)
                    if tc == 1 and c + 1 < NC:
                        kt_transp(c + 1)
                    if c >= 1:
                        kp = 4 * (c - 1) + tc
                        tensor.wait_ge(m_sq, kp + 1)
                        tensor.matmul(P[tc][0:1, :], onescol[:, :], sqb[kp % 4][:, :],
                                      start=(c - 1 == 0), stop=False).then_inc(m_ones, 1)
            for tc in range(TCH):
                kp = 4 * (NC - 1) + tc
                tensor.wait_ge(m_sq, kp + 1)
                tensor.matmul(P[tc][0:1, :], onescol[:, :], sqb[kp % 4][:, :],
                              start=False, stop=True).then_inc(m_ones, 1)
            tensor.wait_ge(m_nl, 2)
            for tc in range(TCH):
                tensor.matmul(P[tc][:, :], onesr2, nrow[:, 512 * tc:512 * (tc + 1)],
                              start=True, stop=True).then_inc(m_bc, 1)
            tensor.wait_ge(m_ktsb, NC)
            WP = [P6b, P7b]
            for wr in range(8):
                tensor.wait_ge(m_w, 16 * (wr + 1))
                if wr >= 2:
                    tensor.wait_ge(m_wev, wr - 1)
                for ii in range(8):
                    mm = tensor.transpose(WP[wr % 2][:, 128 * ii:128 * (ii + 1)],
                                          wstage[wr % 6][:, 128 * ii:128 * (ii + 1)], ident[:, :])
                    if ii == 7:
                        mm.then_inc(m_wtt, 1)
            for f in TOPS:
                f(tensor)

        @block.vector
        def _(vector):
            vector.memset(epsb[:, :], 1e-5)
            vector.memset(onescol[:, :], 1.0)
            vector.memset(intc[:, 0:1], SIGN_MASK)
            vector.memset(intc[:, 1:2], EXP_MASK)
            vector.memset(cbc[:, 0:1], EXP_BIAS_C)
            vector.memset(cbc[:, 1:2], -0.5)
            vector.memset(onesr, 1.0)
            vector.wait_ge(m_pro, 48)
            vector.tensor_tensor(sval, isv, wsv, ALU.mult)
            vector.tensor_copy(spc[:, :], epsb[:, :]).then_inc(m_prov, 1)
            vector.wait_ge(m_prot, 1)
            vector.tensor_copy(spc[:, :], epsb[:, :])
            vector.tensor_copy(spc[:, :], epsb[:, :])
            vector.tensor_copy(recip_b[:, :], P3[:, 0:1])
            vector.tensor_copy(biasb[:, 0:512], P0[:, :])
            vector.tensor_copy(biasb[:, 512:1024], P1[:, :])
            vector.tensor_copy(s_b[:, :], P2[:, 0:1])
            vector.reciprocal(recip_b[:, :], recip_b[:, :])
            vector.tensor_copy(spc[:, :], epsb[:, :])
            vector.tensor_copy(spc[:, :], epsb[:, :]).then_inc(m_prov, 1)
            for i in range(NTILE + 2):
                if i < NTILE:
                    sl = i % 2
                    vector.wait_ge(m_sc, SC0[("acc", i)])
                    vector.tensor_scalar(s0[("mean", sl)], s0[("sums", sl)], 1.0 / D, None, ALU.mult)
                    vector.tensor_scalar(s0[("var", sl)], s0[("sumsq", sl)], 1.0 / D, None, ALU.mult)
                    vector.tensor_scalar(s0[("std", sl)], s0[("mean", sl)], s0[("mean", sl)], None, ALU.mult)
                    vector.tensor_copy(spc[:, :], epsb[:, :])
                    vector.tensor_sub(s0[("var", sl)], s0[("var", sl)], s0[("std", sl)])
                    vector.tensor_copy(spc[:, :], epsb[:, :]).then_inc(m_ve, 1)
                if 1 <= i <= NTILE:
                    j = i - 1
                    sl = j % 2
                    vector.wait_ge(m_sc, SC0[("sqrt", j)])
                    vector.reciprocal(s0[("rstd", sl)], s0[("std", sl)])
                    vector.tensor_scalar(s0[("nmr", sl)], s0[("mean", sl)], -1.0, None, ALU.mult)
                    vector.tensor_copy(spc[:, :], epsb[:, :])
                    vector.tensor_tensor(s0[("nmr", sl)], s0[("nmr", sl)], s0[("rstd", sl)], ALU.mult)
                    vector.tensor_copy(spc[:, :], epsb[:, :]).then_inc(m_ve, 1)
                if i >= 2:
                    j2 = i - 2
                    vector.wait_ge(m_tt, j2 + 1)
                    vector.tensor_copy(spc[:, :], epsb[:, :])
                    vector.tensor_copy(spc[:, :], epsb[:, :])
                    vector.tensor_copy(hT[:, :, 128 * j2:128 * (j2 + 1)],
                                       P7b[:, :].rearrange("p (a b) -> p a b", a=8, b=128))
                    vector.tensor_copy(spc[:, :], epsb[:, :]).then_inc(m_ht, 1)
            KTPv = [P6b, P7b]
            def kt_copy(cc):
                vector.wait_ge(m_kt, cc + 1)
                vector.tensor_copy(spc[:, :], epsb[:, :])
                vector.tensor_copy(spc[:, :], epsb[:, :])
                vector.tensor_copy(KT_sb[cc % 2][:, :], KTPv[cc % 2][:, :])
                vector.tensor_copy(spc[:, :], epsb[:, :]).then_inc(m_ktsb, 1)
            kt_copy(0)
            for c in range(NC):
                for tc in range(TCH):
                    k = 4 * c + tc
                    vector.wait_ge(m_mm1, k + 1)
                    if c >= 2 and tc == 0:
                        vector.wait_ge(m_spill, 16 * (c - 1))
                    vector.tensor_copy(spc[:, :], epsb[:, :])
                    vector.tensor_copy(spc[:, :], epsb[:, :])
                    vector.tensor_copy(s_sb[c % 2][:, 512 * tc:512 * (tc + 1)],
                                       P[4 + k % 2][:, :])
                    vector.tensor_copy(spc[:, :], epsb[:, :]).then_inc(m_sevac, 1)
                    if tc == 1 and c + 1 < NC:
                        kt_copy(c + 1)
            vector.wait_ge(m_ones, 4 * NC)
            vector.wait_ge(m_spill, 16 * NC)
            vector.memset(onesr2, 1.0)
            vector.tensor_copy(spc[:, :], epsb[:, :])
            vector.tensor_copy(spc[:, :], epsb[:, :])
            for tc in range(TCH):
                vector.tensor_copy(nrow[:, 512 * tc:512 * (tc + 1)], P[tc][0:1, :])
            vector.tensor_copy(spc[:, :], epsb[:, :])
            vector.tensor_copy(spc[:, :], epsb[:, :]).then_inc(m_nr, 1)
            vector.wait_ge(m_nl, 1)
            vector.reciprocal(nrow[:, :], nrow[:, :])
            vector.tensor_copy(spc[:, :], epsb[:, :])
            vector.tensor_scalar(nrow[:, :], nrow[:, :], float(np.sqrt(8192.0)), None, ALU.mult)
            vector.tensor_copy(spc[:, :], epsb[:, :]).then_inc(m_nl, 1)
            for tc in range(TCH):
                vector.wait_ge(m_bc, tc + 1)
                vector.tensor_copy(spc[:, :], epsb[:, :])
                vector.tensor_copy(spc[:, :], epsb[:, :])
                vector.tensor_copy(scB[:, 512 * tc:512 * (tc + 1)], P[tc][:, :])
                vector.tensor_copy(spc[:, :], epsb[:, :]).then_inc(m_scb, 1)
            WPv = [P6b, P7b]
            for wr in range(8):
                vector.wait_ge(m_wtt, wr + 1)
                vector.tensor_copy(spc[:, :], epsb[:, :])
                vector.tensor_copy(spc[:, :], epsb[:, :])
                vector.tensor_copy(WT[:, :, 128 * wr:128 * (wr + 1)],
                                   WPv[wr % 2][:, :].rearrange("p (a b) -> p a b", a=8, b=128))
                vector.tensor_copy(spc[:, :], epsb[:, :]).then_inc(m_wev, 1)
            for f in VOPS:
                f(vector)


        @block.scalar
        def _(scalar):
            for i in range(NTILE + 2):
                if i < NTILE:
                    sl = i % 2
                    scalar.wait_ge(m_x, 16 * (i + 1))
                    if i >= 2:
                        scalar.wait_ge(m_ve, VE0[("stats", i - 2)])
                    scalar.activation(dumpB[:, :], xstage[i % 4][:, :], AF.Identity,
                                      accum_out=s0[("sums", sl)])
                    scalar.activation(dumpB[:, :], xstage[i % 4][:, :], AF.Square,
                                      accum_out=s0[("sumsq", sl)])
                    scalar.copy(spcS[:, :], epsb[:, :]).then_inc(m_sc, 1)
                if 1 <= i <= NTILE:
                    j = i - 1
                    sl = j % 2
                    scalar.wait_ge(m_ve, VE0[("stats", j)])
                    scalar.activation(s0[("std", sl)], s0[("var", sl)], AF.Sqrt,
                                      bias=epsb[:, 0:1], scale=1.0)
                    scalar.copy(spcS[:, :], epsb[:, :]).then_inc(m_sc, 1)
                if i >= 2:
                    j = i - 2
                    sl = j % 2
                    scalar.wait_ge(m_ve, VE0[("rn", j)])
                    if j >= 2:
                        scalar.wait_ge(m_tt, j - 1)
                    scalar.activation(hnorm[j % 2][:, :], xstage[j % 4][:, :], AF.Identity,
                                      bias=s0[("nmr", sl)], scale=s0[("rstd", sl)])
                    scalar.copy(spcS[:, :], epsb[:, :]).then_inc(m_sc, 1)
            for c in range(NC):
                for tc in range(TCH):
                    k = 4 * c + tc
                    scalar.wait_ge(m_sevac, k + 1)
                    if k >= 4:
                        scalar.wait_ge(m_ones, k - 3)
                    scalar.activation(sqb[k % 4][:, :], s_sb[c % 2][:, 512 * tc:512 * (tc + 1)],
                                      AF.Square).then_inc(m_sq, 1)
            scalar.wait_ge(m_nr, 1)
            scalar.activation(nrow, nrow, AF.Sqrt)
            scalar.copy(spcS[:, :], epsb[:, :]).then_inc(m_nl, 1)
            for f in SOPS:
                f(scalar)
            scalar.wait_ge(m_out, 16 * 8 * NG)

    _es.close()
    return nc


_NC_CACHE = [None]


def kernel(L0_attn_q, L0_attn_k, L0_attn_v, L0_ffn_weight_fp8, L0_ffn_bias,
           L0_ffn_input_scale, L0_ffn_weight_scale):
    q = np.ascontiguousarray(np.asarray(L0_attn_q, dtype=np.float32))
    k = np.ascontiguousarray(np.asarray(L0_attn_k, dtype=np.float32))
    v = np.ascontiguousarray(np.asarray(L0_attn_v, dtype=np.float32))
    w = np.ascontiguousarray(np.asarray(L0_ffn_weight_fp8, dtype=np.float32))
    b = np.ascontiguousarray(np.asarray(L0_ffn_bias, dtype=np.float32)).reshape(1, O)
    isc = np.ascontiguousarray(np.asarray(L0_ffn_input_scale, dtype=np.float32)).reshape(1, 1)
    wsc = np.ascontiguousarray(np.asarray(L0_ffn_weight_scale, dtype=np.float32)).reshape(1, 1)

    if _NC_CACHE[0] is None:
        _NC_CACHE[0] = build_program()
    nc = _NC_CACHE[0]

    in_maps = [{"x": q[core], "kk": k, "vv": v, "ww": w, "bias": b,
                "iscale": isc, "wscale": wsc} for core in range(8)]
    res = run_bass_kernel_spmd(nc, in_maps, core_ids=list(range(8)))
    out = np.stack([res.results[i]["out"] for i in range(8)], axis=0)
    return out.astype(np.float32)


# version 23 (final)
# speedup vs baseline: 1.0588x; 1.0588x over previous
import math
import numpy as np
import concourse.bass as bass
import concourse.mybir as mybir
from concourse.bass_utils import run_bass_kernel_spmd

F32 = mybir.dt.float32
BF16 = mybir.dt.bfloat16
I32 = mybir.dt.int32
I8 = mybir.dt.int8
AF = mybir.ActivationFunctionType
ALU = mybir.AluOpType

T = 2048
D = 1024
N = 8192
NC = 64
O = 1024
NTILE = 16
TCH = 4
NG = 4
KU = 32
VU = 16
SSU = 16

SIGN_MASK = int(np.uint32(0x80000000).view(np.int32))
EXP_MASK = 0x7F800000
FP8_MAX = 448.0
MAGIC_MULT = float(1.5 * 2.0 ** 20)
EMIN = float(2.0 ** -6)
EXP_BIAS_C = float(0.5 * math.log(8192.0))


def _gelval(zi):
    return 129 * (zi // 128) + (zi % 128) + 1


def build_program(debug=False):
    nc = bass.Bass("TRN2", target_bir_lowering=False)

    x_hbm = nc.dram_tensor("x", [T, D], F32, kind="ExternalInput").ap()
    k_hbm = nc.dram_tensor("kk", [N, D], F32, kind="ExternalInput").ap()
    v_hbm = nc.dram_tensor("vv", [N, O], F32, kind="ExternalInput").ap()
    w_hbm = nc.dram_tensor("ww", [O, D], F32, kind="ExternalInput").ap()
    b_hbm = nc.dram_tensor("bias", [1, O], F32, kind="ExternalInput").ap()
    is_hbm = nc.dram_tensor("iscale", [1, 1], F32, kind="ExternalInput").ap()
    ws_hbm = nc.dram_tensor("wscale", [1, 1], F32, kind="ExternalInput").ap()
    out_hbm = nc.dram_tensor("out", [T, D], F32, kind="ExternalOutput").ap()
    spill = nc.dram_tensor("spill", [NC, 8, 128, 256], BF16,
                           kind="ExternalOutput" if debug else "Internal").ap()
    padscr = nc.dram_tensor("padscr", [128, 256], BF16, kind="Internal").ap()
    if debug:
        dbg_ht = nc.dram_tensor("dbg_ht", [128, 8, T], F32, kind="ExternalOutput").ap()
        dbg_scb = nc.dram_tensor("dbg_scb", [128, T], F32, kind="ExternalOutput").ap()

    import ml_dtypes
    ident_dram = nc.inline_tensor(np.eye(128, dtype=ml_dtypes.bfloat16), name="ident").ap()


    from contextlib import ExitStack
    _es = ExitStack()
    RV = _es.enter_context(nc.sbuf_tensor("RV", [128, 131072], I8))
    RA = _es.enter_context(nc.sbuf_tensor("RA", [128, 32768], I8))
    RB = _es.enter_context(nc.sbuf_tensor("RB", [128, 32768], I8))
    scB = _es.enter_context(nc.sbuf_tensor("scB", [128, 2048], F32))
    biasb = _es.enter_context(nc.sbuf_tensor("biasb", [128, 1024], F32))
    ident = _es.enter_context(nc.sbuf_tensor("identsb", [128, 128], BF16))
    onescol = _es.enter_context(nc.sbuf_tensor("onescol", [128, 1], BF16))
    epsb = _es.enter_context(nc.sbuf_tensor("epsb", [128, 1], F32))
    s_b = _es.enter_context(nc.sbuf_tensor("s_b", [128, 1], F32))
    recip_b = _es.enter_context(nc.sbuf_tensor("recip_b", [128, 1], F32))
    intc = _es.enter_context(nc.sbuf_tensor("intc", [128, 2], I32))
    cbc = _es.enter_context(nc.sbuf_tensor("cbc", [1, 4], F32))
    sums = _es.enter_context(nc.sbuf_tensor("sums", [128, 1], F32))
    sumsq = _es.enter_context(nc.sbuf_tensor("sumsq", [128, 1], F32))
    meanb = _es.enter_context(nc.sbuf_tensor("meanb", [128, 1], F32))
    varb = _es.enter_context(nc.sbuf_tensor("varb", [128, 1], F32))
    msqb = _es.enter_context(nc.sbuf_tensor("msqb", [128, 1], F32))
    nmsqb = _es.enter_context(nc.sbuf_tensor("nmsqb", [128, 1], F32))
    s0st = _es.enter_context(nc.sbuf_tensor("s0st", [128, 14], F32))
    stdb = _es.enter_context(nc.sbuf_tensor("stdb", [128, 1], F32))
    rstdb = _es.enter_context(nc.sbuf_tensor("rstdb", [128, 1], F32))
    nmrb = _es.enter_context(nc.sbuf_tensor("nmrb", [128, 1], F32))
    aapb = _es.enter_context(nc.sbuf_tensor("aapb", [128, 1], F32))
    bapb = _es.enter_context(nc.sbuf_tensor("bapb", [128, 1], F32))
    P0 = _es.enter_context(nc.psum_tensor("P0", [128, 512], F32))
    P1 = _es.enter_context(nc.psum_tensor("P1", [128, 512], F32))
    P2 = _es.enter_context(nc.psum_tensor("P2", [128, 512], F32))
    P3 = _es.enter_context(nc.psum_tensor("P3", [128, 512], F32))
    P4 = _es.enter_context(nc.psum_tensor("P4", [128, 512], F32))
    P5 = _es.enter_context(nc.psum_tensor("P5", [128, 512], F32))
    P6 = _es.enter_context(nc.psum_tensor("P6", [128, 512], F32))
    P7 = _es.enter_context(nc.psum_tensor("P7", [128, 512], F32))
    m_pro = _es.enter_context(nc.semaphore("m_pro"))
    m_prov = _es.enter_context(nc.semaphore("m_prov"))
    m_prot = _es.enter_context(nc.semaphore("m_prot"))
    m_x = _es.enter_context(nc.semaphore("m_x"))
    m_ve = _es.enter_context(nc.semaphore("m_ve"))
    m_sc = _es.enter_context(nc.semaphore("m_sc"))
    m_tt = _es.enter_context(nc.semaphore("m_tt"))
    m_ht = _es.enter_context(nc.semaphore("m_ht"))
    m_k = _es.enter_context(nc.semaphore("m_k"))
    m_kt = _es.enter_context(nc.semaphore("m_kt"))
    m_ktsb = _es.enter_context(nc.semaphore("m_ktsb"))
    m_mm1 = _es.enter_context(nc.semaphore("m_mm1"))
    m_sevac = _es.enter_context(nc.semaphore("m_sevac"))
    m_sq = _es.enter_context(nc.semaphore("m_sq"))
    m_ones = _es.enter_context(nc.semaphore("m_ones"))
    m_spill = _es.enter_context(nc.semaphore("m_spill"))
    m_v = _es.enter_context(nc.semaphore("m_v"))
    m_nr = _es.enter_context(nc.semaphore("m_nr"))
    m_nl = _es.enter_context(nc.semaphore("m_nl"))
    m_bc = _es.enter_context(nc.semaphore("m_bc"))
    m_scb = _es.enter_context(nc.semaphore("m_scb"))
    m_w = _es.enter_context(nc.semaphore("m_w"))
    m_wtt = _es.enter_context(nc.semaphore("m_wtt"))
    m_wev = _es.enter_context(nc.semaphore("m_wev"))
    m_xg = _es.enter_context(nc.semaphore("m_xg"))
    m_ssl = _es.enter_context(nc.semaphore("m_ssl"))
    m_ssl2 = _es.enter_context(nc.semaphore("m_ssl2"))
    m_z = _es.enter_context(nc.semaphore("m_z"))
    m_gel = _es.enter_context(nc.semaphore("m_gel"))
    m_mm2 = _es.enter_context(nc.semaphore("m_mm2"))
    m_bv = _es.enter_context(nc.semaphore("m_bv"))
    m_bs = _es.enter_context(nc.semaphore("m_bs"))
    m_bt = _es.enter_context(nc.semaphore("m_bt"))
    m_out = _es.enter_context(nc.semaphore("m_out"))
    spc = _es.enter_context(nc.sbuf_tensor("spc", [128, 1], F32))
    spcS = _es.enter_context(nc.sbuf_tensor("spcS", [128, 1], F32))
    block = _es.enter_context(nc.Block())
    if True:
        P = [P0, P1, P2, P3, P4, P5, P6, P7]

        def carve(region, off, p0, p1, dtype, shape2):
            nelem = 1
            for s in shape2:
                nelem *= s
            nbytes = nelem * mybir.dt.size(dtype)
            ap = region.ap()[p0:p1, off:off + nbytes].bitcast(dtype)
            if len(shape2) > 1:
                letters = "abcdefg"[: len(shape2)]
                ap = ap.rearrange(
                    "p (" + " ".join(letters) + ") -> p " + " ".join(letters),
                    **{letters[i]: shape2[i] for i in range(len(shape2))},
                )
            return ap

        V_sb = carve(RV, 0, 0, 128, BF16, [NC, 1024])
        hT = carve(RA, 0, 0, 128, BF16, [8, 2048])
        xg = carve(RA, 0, 0, 128, F32, [4, 1024])
        WT = carve(RA, 16384, 0, 128, BF16, [8, 1024])
        xstage = [carve(RB, 0, 0, 128, F32, [1024]), carve(RB, 4096, 0, 128, F32, [1024]),
                  carve(RV, 114688, 0, 128, F32, [1024]), carve(RV, 118784, 0, 128, F32, [1024])]
        dumpA = carve(RB, 8192, 0, 128, BF16, [1024])
        dumpB = carve(RB, 10240, 0, 128, BF16, [1024])
        hnorm = [carve(RB, 20480, 0, 128, BF16, [1024]), carve(RB, 22528, 0, 128, BF16, [1024])]
        bias_row = carve(RB, 12288, 0, 1, F32, [1024])
        onesr = carve(RB, 16384, 0, 1, F32, [128])
        isv = carve(RB, 16896, 0, 1, F32, [1])
        wsv = carve(RB, 16960, 0, 1, F32, [1])
        sval = carve(RB, 17024, 0, 1, F32, [1])
        kstage = [carve(RV, 122880, 0, 128, BF16, [2, 1024]), carve(RV, 126976, 0, 128, BF16, [2, 1024])]
        s_sb = [carve(RB, 12288, 0, 128, BF16, [2048]), carve(RB, 16384, 0, 128, BF16, [2048])]
        sqb = [carve(RB, 20480 + 1024 * i, 0, 128, BF16, [512]) for i in range(4)]
        KT_sb = [carve(RB, 24576, 0, 128, BF16, [1024]), carve(RB, 26624, 0, 128, BF16, [1024])]
        nrow = carve(RB, 0, 0, 1, F32, [2048])
        onesr2 = carve(RB, 8192, 0, 1, F32, [128])
        ssl = [carve(RB, 2048 * i, 0, 128, BF16, [4, 256]) for i in range(5)]
        obuf = [carve(RB, 10240 + 2048 * i, 0, 128, F32, [512]) for i in range(2)]
        zb = [carve(RB, 14336 + 1024 * i, 0, 128, F32, [256]) for i in range(4)]
        gbuf = [carve(RB, 18432 + 512 * i, 0, 128, BF16, [256]) for i in range(4)]
        wstage = [carve(RB, 20480, 0, 128, BF16, [1024]), carve(RB, 22528, 0, 128, BF16, [1024])]
        qT = carve(RB, 20480, 0, 128, BF16, [8, 256])
        qC = carve(RB, 24576, 0, 128, I32, [512])
        qS = carve(RB, 26624, 0, 128, I32, [512])
        qM = carve(RB, 28672, 0, 128, I32, [512])
        qCf = carve(RB, 24576, 0, 128, F32, [512])
        qSf = carve(RB, 26624, 0, 128, F32, [512])
        qMf = carve(RB, 28672, 0, 128, F32, [512])
        qb = carve(RB, 30720, 0, 128, BF16, [1024])
        dumpB2 = carve(RB, 30720, 0, 128, BF16, [1024])
        P6b = P6.ap().bitcast(BF16)
        P7b = P7.ap().bitcast(BF16)

        from collections import deque
        SC0, VE0 = {}, {}
        _c0 = 0
        for _i in range(NTILE + 2):
            if _i < NTILE:
                _c0 += 1; SC0[("acc", _i)] = _c0
            if 1 <= _i <= NTILE:
                _c0 += 1; SC0[("sqrt", _i - 1)] = _c0
            if _i >= 2:
                _c0 += 1; SC0[("hnorm", _i - 2)] = _c0
        _c0 = 0
        for _i in range(NTILE + 2):
            if _i < NTILE:
                _c0 += 1; VE0[("stats", _i)] = _c0
            if 1 <= _i <= NTILE:
                _c0 += 1; VE0[("rn", _i - 1)] = _c0
        s0 = {}
        for _k, _off in (("sums", 0), ("sumsq", 2), ("mean", 4), ("var", 6),
                         ("std", 8), ("rstd", 10), ("nmr", 12)):
            for _s in range(2):
                s0[(_k, _s)] = s0st[:, _off + _s:_off + _s + 1]

        VOPS, SOPS, TOPS = [], [], []
        BVv, BSv, BTv = {}, {}, {}
        _cnt = {"bv": 0, "bs": 0, "bt": 0}
        vq, squ, tq = deque(), deque(), deque()

        def _lazy(d, key):
            return lambda: d[key]

        def v_z(g, p, c):
            zi = 128 * g + 64 * p + c
            Pi = 2 * g + p
            def f(vec):
                vec.wait_ge(m_ssl, 16 * (17 * Pi + min(c // 4 + 3, 17)))
                if zi >= 4:
                    vec.wait_ge(m_gel, _gelval(zi - 4))
                vec.tensor_tensor(zb[c % 4][:, :], ssl[(16 * Pi + c // 4) % 5][:, c % 4, :],
                                  scB[:, 512 * g + 256 * p:512 * g + 256 * p + 256], ALU.mult)
                vec.tensor_copy(spc[:, :], epsb[:, :]).then_inc(m_z, 1)
            VOPS.append(f)

        def v_xevac(g, p):
            _cnt["bv"] += 1
            BVv[("x2p", g, p)] = _cnt["bv"]
            def f(vec):
                vec.wait_ge(m_mm2, 128 * g + 64 * (p + 1))
                vec.wait_ge(m_xg, 32 * g + 16 * (p + 1))
                for j in range(4):
                    tb = 2 * p + j // 2
                    oh = j % 2
                    vec.tensor_add(xg[:, tb, 512 * oh:512 * (oh + 1)], P[j][:, :],
                                   xg[:, tb, 512 * oh:512 * (oh + 1)])
                vec.tensor_copy(spc[:, :], epsb[:, :])
                vec.tensor_copy(spc[:, :], epsb[:, :]).then_inc(m_bv, 1)
            VOPS.append(f)


        def vb_aap(g, tb):
            def mk():
                w = _lazy(BSv, ("std", g, tb))
                def f(vec):
                    vec.wait_ge(m_bs, w())
                    vec.reciprocal(rstdb[:, :], stdb[:, :])
                    vec.tensor_scalar(nmrb[:, :], meanb[:, :], -1.0, None, ALU.mult)
                    vec.tensor_tensor(aapb[:, :], rstdb[:, :], recip_b[:, :], ALU.mult)
                    vec.tensor_tensor(nmrb[:, :], nmrb[:, :], rstdb[:, :], ALU.mult)
                    vec.tensor_copy(spc[:, :], epsb[:, :])
                    vec.tensor_tensor(bapb[:, :], nmrb[:, :], recip_b[:, :], ALU.mult)
                    vec.tensor_copy(spc[:, :], epsb[:, :])
                VOPS.append(f)
            return mk

        def vb_norm(g, tb, h):
            def mk():
                def f(vec):
                    vec.tensor_scalar(qCf[:, 0:256], xg[:, tb, 512 * h:512 * h + 256],
                                      aapb[:, 0:1], bapb[:, 0:1], ALU.mult, ALU.add)
                    vec.tensor_scalar(qCf[:, 256:512], xg[:, tb, 512 * h + 256:512 * (h + 1)],
                                      aapb[:, 0:1], bapb[:, 0:1], ALU.mult, ALU.add)
                VOPS.append(f)
            return mk

        def vb_qbA(g, tb, h):
            def mk():
                def f(vec):
                    qsl = [slice(0, 256), slice(256, 512)]
                    for sl in qsl:
                        vec.tensor_scalar(qCf[:, sl], qCf[:, sl], -FP8_MAX, FP8_MAX,
                                          ALU.max, ALU.min)
                    for sl in qsl:
                        vec.tensor_scalar(qM[:, sl], qC[:, sl], intc[:, 1:2], None, ALU.bitwise_and)
                    for sl in qsl:
                        vec.tensor_scalar(qS[:, sl], qC[:, sl], intc[:, 0:1], None, ALU.bitwise_and)
                VOPS.append(f)
            return mk

        def vb_qbB(g, tb, h):
            def mk():
                def f(vec):
                    qsl = [slice(0, 256), slice(256, 512)]
                    for sl in qsl:
                        vec.tensor_scalar(qMf[:, sl], qMf[:, sl], EMIN, MAGIC_MULT,
                                          ALU.max, ALU.mult)
                    for sl in qsl:
                        vec.tensor_tensor(qS[:, sl], qM[:, sl], qS[:, sl], ALU.bitwise_or)
                    for sl in qsl:
                        vec.tensor_tensor(qMf[:, sl], qCf[:, sl], qSf[:, sl], ALU.add)
                VOPS.append(f)
            return mk

        def vb_qbC(g, tb, h):
            def mk():
                _cnt["bv"] += 1
                BVv[("qbsub", g, tb, h)] = _cnt["bv"]
                pq = (g, tb - 1) if tb >= 1 else ((g - 1, 3) if g >= 1 else None)
                wq = _lazy(BTv, ("qtt",) + pq) if (h == 0 and pq is not None) else None
                def f(vec):
                    if wq is not None:
                        vec.wait_ge(m_bt, wq())
                    vec.tensor_tensor(qb[:, 512 * h:512 * h + 256], qMf[:, 0:256],
                                      qSf[:, 0:256], ALU.subtract)
                    vec.tensor_tensor(qb[:, 512 * h + 256:512 * (h + 1)], qMf[:, 256:512],
                                      qSf[:, 256:512], ALU.subtract)
                    vec.tensor_copy(spc[:, :], epsb[:, :]).then_inc(m_bv, 1)
                VOPS.append(f)
            return mk

        def sb_qtev(g, tb):
            def mk():
                _cnt["bs"] += 1
                BSv[("qtev", g, tb)] = _cnt["bs"]
                w = _lazy(BTv, ("qtt", g, tb))
                def f(sca):
                    sca.wait_ge(m_bt, w())
                    sca.copy(qT[:, :, 128 * (tb % 2):128 * (tb % 2) + 128],
                             P6b[:, 0:1024].rearrange("p (a b) -> p a b", a=8, b=128))
                    sca.copy(spcS[:, :], epsb[:, :]).then_inc(m_bs, 1)
                SOPS.append(f)
            return mk

        def vb_epiA(g, tb2):
            def mk():
                w = _lazy(BTv, ("ffn", g, tb2, 1))
                def f(vec):
                    vec.wait_ge(m_bt, w())
                    for oh in range(2):
                        e = 8 * g + 2 * tb2 + oh
                        if e >= 2:
                            vec.wait_ge(m_out, 16 * (e - 1))
                        vec.tensor_add(obuf[e % 2][:, :], xg[:, tb2, 512 * oh:512 * (oh + 1)],
                                       biasb[:, 512 * oh:512 * (oh + 1)])
                VOPS.append(f)
            return mk

        def vb_epiB(g, tb2):
            def mk():
                for oh in range(2):
                    _cnt["bv"] += 1
                    BVv[("epi", g, tb2, oh)] = _cnt["bv"]
                def f(vec):
                    for oh in range(2):
                        e = 8 * g + 2 * tb2 + oh
                        vec.scalar_tensor_tensor(obuf[e % 2][:, :], P[4 + e % 2][:, :],
                                                 s_b[:, 0:1], obuf[e % 2][:, :],
                                                 ALU.mult, ALU.add)
                    vec.tensor_copy(spc[:, :], epsb[:, :]).then_inc(m_bv, 1)
                    vec.tensor_copy(spc[:, :], epsb[:, :]).then_inc(m_bv, 1)
                VOPS.append(f)
            return mk

        def s_gelu(g, p, c):
            zi = 128 * g + 64 * p + c
            def f(sca):
                sca.wait_ge(m_z, zi + 1)
                if zi >= 4:
                    sca.wait_ge(m_mm2, zi - 3)
                sca.activation(gbuf[c % 4][:, :], zb[c % 4][:, :], AF.Gelu)
                sca.copy(spcS[:, :], epsb[:, :]).then_inc(m_gel, 1)
            SOPS.append(f)

        def s_geltail(g):
            def f(sca):
                sca.copy(spcS[:, :], epsb[:, :]).then_inc(m_gel, 1)
            SOPS.append(f)

        def sb_sumsqA(g, tb):
            def mk():
                w = _lazy(BVv, ("x2p", g, tb // 2))
                pq = (g, tb - 1) if tb >= 1 else ((g - 1, 3) if g >= 1 else None)
                wq = _lazy(BTv, ("qtt",) + pq) if pq is not None else None
                def f(sca):
                    sca.wait_ge(m_bv, w())
                    if wq is not None:
                        sca.wait_ge(m_bt, wq())
                    sca.activation(dumpB2[:, :], xg[:, tb, :], AF.Identity, accum_out=sums[:, :])
                SOPS.append(f)
            return mk

        def sb_sumsqB(g, tb):
            def mk():
                _cnt["bs"] += 1
                BSv[("std", g, tb)] = _cnt["bs"]
                def f(sca):
                    sca.activation(dumpB2[:, :], xg[:, tb, :], AF.Square, accum_out=sumsq[:, :])
                    sca.activation(meanb[:, :], sums[:, :], AF.Identity, scale=1.0 / D)
                    sca.copy(spcS[:, :], epsb[:, :])
                    sca.activation(msqb[:, :], meanb[:, :], AF.Square)
                    sca.copy(spcS[:, :], epsb[:, :])
                    sca.activation(nmsqb[:, :], msqb[:, :], AF.Identity,
                                   bias=epsb[:, 0:1], scale=-1.0)
                    sca.copy(spcS[:, :], epsb[:, :])
                    sca.activation(stdb[:, :], sumsq[:, :], AF.Sqrt,
                                   bias=nmsqb[:, 0:1], scale=1.0 / D)
                    sca.copy(spcS[:, :], epsb[:, :]).then_inc(m_bs, 1)
                SOPS.append(f)
            return mk

        def sb_out(g, tb2):
            def mk():
                w = _lazy(BVv, ("epi", g, tb2, 1))
                def f(sca):
                    sca.wait_ge(m_bv, w())
                    for oh in range(2):
                        e = 8 * g + 2 * tb2 + oh
                        sca.dma_start(out=out_hbm[512 * g + 128 * tb2:512 * g + 128 * (tb2 + 1),
                                                  512 * oh:512 * (oh + 1)],
                                      in_=obuf[e % 2][:, :]).then_inc(m_out, 16)
                SOPS.append(f)
            return mk

        def t_mm2(g, p, c):
            zi = 128 * g + 64 * p + c
            def f(ten):
                ten.wait_ge(m_gel, min(_gelval(zi + 1), 129 * (g + 1)))
                if p == 0 and c == 0:
                    if g == 0:
                        ten.wait_ge(m_scb, 4)
                        ten.wait_ge(m_wev, 8)
                    else:
                        ten.wait_ge(m_bv, BVv[("x2p", g - 1, 1)])
                if p == 1 and c == 0:
                    ten.wait_ge(m_bv, BVv[("x2p", g, 0)])
                if g == 0 and p == 0:
                    ten.wait_ge(m_v, 16 * (c // 4 + 1))
                for tbL in range(2):
                    for oh in range(2):
                        mm = ten.matmul(P[2 * tbL + oh][:, :],
                                        gbuf[c % 4][:, 128 * tbL:128 * (tbL + 1)],
                                        V_sb[:, c, 512 * oh:512 * (oh + 1)],
                                        start=(c == 0), stop=(c == NC - 1))
                        if tbL == 1 and oh == 1:
                            mm.then_inc(m_mm2, 1)
            TOPS.append(f)

        def tb_qtt(g, tb):
            def mk():
                _cnt["bt"] += 1
                BTv[("qtt", g, tb)] = _cnt["bt"]
                w = _lazy(BVv, ("qbsub", g, tb, 1))
                pq = (g, tb - 1) if tb >= 1 else ((g - 1, 3) if g >= 1 else None)
                wq = _lazy(BSv, ("qtev",) + pq) if pq is not None else None
                def f(ten):
                    ten.wait_ge(m_bv, w())
                    if wq is not None:
                        ten.wait_ge(m_bs, wq())
                    for ii in range(8):
                        mm = ten.transpose(P6b[:, 128 * ii:128 * (ii + 1)],
                                           qb[:, 128 * ii:128 * (ii + 1)], ident[:, :])
                        if ii == 7:
                            mm.then_inc(m_bt, 1)
                TOPS.append(f)
            return mk

        def tb_ffn(g, tb2):
            def mk():
                vals = []
                for oh in range(2):
                    _cnt["bt"] += 1
                    BTv[("ffn", g, tb2, oh)] = _cnt["bt"]
                w = _lazy(BSv, ("qtev", g, tb2))
                eps_w = []
                for oh in range(2):
                    e = 8 * g + 2 * tb2 + oh
                    if e >= 2:
                        ep = e - 2
                        gg, r = divmod(ep, 8)
                        eps_w.append(_lazy(BVv, ("epi", gg, r // 2, r % 2)))
                    else:
                        eps_w.append(None)
                def f(ten):
                    ten.wait_ge(m_bs, w())
                    for oh in range(2):
                        e = 8 * g + 2 * tb2 + oh
                        if eps_w[oh] is not None:
                            ten.wait_ge(m_bv, eps_w[oh]())
                        for ii in range(8):
                            mm = ten.matmul(P[4 + e % 2][:, :],
                                            qT[:, ii, 128 * (tb2 % 2):128 * (tb2 % 2) + 128],
                                            WT[:, ii, 512 * oh:512 * (oh + 1)],
                                            start=(ii == 0), stop=(ii == 7))
                            if ii == 7:
                                mm.then_inc(m_bt, 1)
                TOPS.append(f)
            return mk

        def _queue_B(g):
            vv, ss, tt = [], [], []
            for tb in range(4):
                b = 64 * (2 * g + 1 + tb // 2) + (12 + 28 * (tb % 2) if tb < 2 else 2 + 28 * (tb % 2))
                ss.append((b + 0, sb_sumsqA(g, tb)))
                ss.append((b + 1, sb_sumsqB(g, tb)))
                vv.append((b + 4, vb_aap(g, tb)))
                vv.append((b + 5, vb_norm(g, tb, 0)))
                vv.append((b + 6, vb_qbA(g, tb, 0)))
                vv.append((b + 7, vb_qbB(g, tb, 0)))
                vv.append((b + 8, vb_qbC(g, tb, 0)))
                vv.append((b + 9, vb_norm(g, tb, 1)))
                vv.append((b + 10, vb_qbA(g, tb, 1)))
                vv.append((b + 11, vb_qbB(g, tb, 1)))
                vv.append((b + 12, vb_qbC(g, tb, 1)))
                tt.append((b + 13, tb_qtt(g, tb)))
                ss.append((b + 19, sb_qtev(g, tb)))
                tt.append((b + 21, tb_ffn(g, tb)))
                vv.append((b + 30, vb_epiA(g, tb)))
                vv.append((b + 31, vb_epiB(g, tb)))
                ss.append((b + 34, sb_out(g, tb)))
            vv.sort(key=lambda x: x[0])
            ss.sort(key=lambda x: x[0])
            tt.sort(key=lambda x: x[0])
            vq.extend(vv)
            squ.extend(ss)
            tq.extend(tt)

        for g in range(NG):
            for p in range(2):
                for c in range(NC):
                    s = 64 * (2 * g + p) + c
                    v_z(g, p, c)
                    if p == 0 and c == 1 and g >= 1:
                        v_xevac(g - 1, 1)
                    while vq and vq[0][0] == s:
                        vq.popleft()[1]()
                    if p == 1 and c == 1:
                        v_xevac(g, 0)
                    s_gelu(g, p, c)
                    while squ and squ[0][0] == s:
                        squ.popleft()[1]()
                    t_mm2(g, p, c)
                    while tq and tq[0][0] == s:
                        tq.popleft()[1]()
                if p == 0:
                    _queue_B(g)
            s_geltail(g)
        v_xevac(NG - 1, 1)
        while vq:
            vq.popleft()[1]()
        while squ:
            squ.popleft()[1]()
        while tq:
            tq.popleft()[1]()

        @block.sync
        def _(sync):
            sync.dma_start(out=bias_row, in_=b_hbm[:, :]).then_inc(m_pro, 16)
            sync.dma_start(out=isv, in_=is_hbm[:, :]).then_inc(m_pro, 16)
            sync.dma_start(out=wsv, in_=ws_hbm[:, :]).then_inc(m_pro, 16)
            sync.dma_start(out=ident[:, :], in_=ident_dram[:, :]).then_inc(m_pro, 16)
            sync.wait_ge(m_prov, 2)
            for i in range(NTILE):
                if i >= 4:
                    sync.wait_ge(m_sc, SC0[("hnorm", i - 4)])
                sync.dma_start(out=xstage[i % 4][:, :],
                               in_=x_hbm[128 * i:128 * (i + 1), :]).then_inc(m_x, 16)
            for c in range(NC):
                sync.wait_ge(m_sevac, 4 * (c + 1))
                sync.dma_start(out=spill[c, :, :, :].transpose([1, 0, 2]),
                               in_=s_sb[c % 2][:, :].rearrange("p (a b) -> p a b", a=8, b=256)
                               ).then_inc(m_spill, 16)
            for g in range(NG):
                for p in range(2):
                    if p == 1:
                        if g >= 1:
                            sync.wait_ge(m_bv, BVv[("epi", g - 1, 1, 1)])
                        sync.dma_start(out=xg[:, 0:2, :],
                                       in_=x_hbm[512 * g:512 * g + 256, :].rearrange("(b p) d -> p b d", p=128)
                                       ).then_inc(m_xg, 16)
                    Pi = 2 * g + p
                    for su in range(SSU):
                        gu = 16 * Pi + su
                        if g == 0 and p == 0 and su == 0:
                            sync.wait_ge(m_bc, 4)
                        if gu >= 5:
                            pgu = gu - 5
                            sync.wait_ge(m_mm2, 64 * (pgu // 16) + 4 * (pgu % 16) + 4)
                        sync.dma_start(out=ssl[gu % 5][:, :, :],
                                       in_=spill[4 * su:4 * su + 4, 2 * g + p, :, :].transpose([1, 0, 2])
                                       ).then_inc(m_ssl, 16)
                    sync.dma_start(out=padscr[0:4, :], in_=spill[0, 0, 0:4, :]).then_inc(m_ssl, 16)
                    if p == 1:
                        if g >= 1:
                            sync.wait_ge(m_bv, BVv[("epi", g - 1, 3, 1)])
                        sync.dma_start(out=xg[:, 2:4, :],
                                       in_=x_hbm[512 * g + 256:512 * (g + 1), :].rearrange("(b p) d -> p b d", p=128)
                                       ).then_inc(m_xg, 16)

        @block.gpsimd
        def _(gpsimd):
            if debug:
                gpsimd.wait_ge(m_ht, NTILE)
                gpsimd.dma_start(out=dbg_ht[:, :, :], in_=hT[:, :, :]).then_inc(m_pro, 16)
            for ku in range(KU):
                if ku >= 2:
                    gpsimd.wait_ge(m_kt, 2 * ku - 2)
                gpsimd.dma_start(out=kstage[ku % 2][:, :, :],
                                 in_=k_hbm[256 * ku:256 * (ku + 1), :].rearrange("(c p) d -> p c d", p=128)
                                 ).then_inc(m_k, 16)
                if ku >= 1 and (ku - 1) < VU - 2:
                    vu = ku - 1
                    gpsimd.dma_start(out=V_sb[:, 4 * vu:4 * (vu + 1), :],
                                     in_=v_hbm[512 * vu:512 * (vu + 1), :].rearrange("(c p) o -> p c o", p=128)
                                     ).then_inc(m_v, 16)
            gpsimd.wait_ge(m_sc, SC0[("hnorm", NTILE - 1)])
            gpsimd.dma_start(out=V_sb[:, 56:60, :],
                             in_=v_hbm[512 * 14:512 * 15, :].rearrange("(c p) o -> p c o", p=128)
                             ).then_inc(m_v, 16)
            gpsimd.wait_ge(m_kt, NC)
            gpsimd.dma_start(out=V_sb[:, 60:64, :],
                             in_=v_hbm[512 * 15:512 * 16, :].rearrange("(c p) o -> p c o", p=128)
                             ).then_inc(m_v, 16)
            gpsimd.wait_ge(m_spill, 16 * NC)
            for wr in range(8):
                if wr >= 2:
                    gpsimd.wait_ge(m_wtt, wr - 1)
                gpsimd.dma_start(out=wstage[wr % 2][:, :],
                                 in_=w_hbm[128 * wr:128 * (wr + 1), :]).then_inc(m_w, 16)
            if debug:
                gpsimd.wait_ge(m_scb, 4)
                gpsimd.dma_start(out=dbg_scb[:, :], in_=scB[:, :]).then_inc(m_pro, 16)

        @block.tensor
        def _(tensor):
            tensor.wait_ge(m_pro, 64)
            tensor.wait_ge(m_prov, 1)
            tensor.matmul(P0[:, :], onesr, bias_row[:, 0:512], start=True, stop=True)
            tensor.matmul(P1[:, :], onesr, bias_row[:, 512:1024], start=True, stop=True)
            tensor.matmul(P2[:, 0:1], onesr, sval, start=True, stop=True)
            tensor.matmul(P3[:, 0:1], onesr, isv, start=True, stop=True).then_inc(m_prot, 1)
            for i in range(NTILE):
                tensor.wait_ge(m_sc, SC0[("hnorm", i)])
                if i >= 1:
                    tensor.wait_ge(m_ht, i)
                for j in range(8):
                    mm = tensor.transpose(P7b[:, 128 * j:128 * (j + 1)],
                                          hnorm[i % 2][:, 128 * j:128 * (j + 1)], ident[:, :])
                    if j == 7:
                        mm.then_inc(m_tt, 1)
            KTP = [P6b, P7b]
            def kt_transp(cc):
                tensor.wait_ge(m_k, 16 * (cc // 2 + 1))
                if cc >= 2:
                    tensor.wait_ge(m_ktsb, cc - 1)
                src = kstage[(cc // 2) % 2][:, cc % 2, :]
                for j in range(8):
                    mm = tensor.transpose(KTP[cc % 2][:, 128 * j:128 * (j + 1)],
                                          src[:, 128 * j:128 * (j + 1)], ident[:, :])
                    if j == 7:
                        mm.then_inc(m_kt, 1)
            kt_transp(0)
            for c in range(NC):
                tensor.wait_ge(m_ktsb, c + 1)
                if c == 0:
                    tensor.wait_ge(m_ht, NTILE)
                for tc in range(TCH):
                    k = 4 * c + tc
                    if k >= 2:
                        tensor.wait_ge(m_sevac, k - 1)
                    bank = P[4 + k % 2]
                    for j in range(8):
                        mm = tensor.matmul(bank[:, :], KT_sb[c % 2][:, 128 * j:128 * (j + 1)],
                                           hT[:, j, 512 * tc:512 * (tc + 1)],
                                           start=(j == 0), stop=(j == 7))
                        if j == 7:
                            mm.then_inc(m_mm1, 1)
                    if tc == 1 and c + 1 < NC:
                        kt_transp(c + 1)
                    if c >= 1:
                        kp = 4 * (c - 1) + tc
                        tensor.wait_ge(m_sq, kp + 1)
                        tensor.matmul(P[tc][0:1, :], onescol[:, :], sqb[kp % 4][:, :],
                                      start=(c - 1 == 0), stop=False).then_inc(m_ones, 1)
            for tc in range(TCH):
                kp = 4 * (NC - 1) + tc
                tensor.wait_ge(m_sq, kp + 1)
                tensor.matmul(P[tc][0:1, :], onescol[:, :], sqb[kp % 4][:, :],
                              start=False, stop=True).then_inc(m_ones, 1)
            tensor.wait_ge(m_nl, 1)
            for tc in range(TCH):
                tensor.matmul(P[tc][:, :], onesr2, nrow[:, 512 * tc:512 * (tc + 1)],
                              start=True, stop=True).then_inc(m_bc, 1)
            tensor.wait_ge(m_ktsb, NC)
            for wr in range(8):
                tensor.wait_ge(m_w, 16 * (wr + 1))
                if wr >= 1:
                    tensor.wait_ge(m_wev, wr)
                for ii in range(8):
                    mm = tensor.transpose(P7b[:, 128 * ii:128 * (ii + 1)],
                                          wstage[wr % 2][:, 128 * ii:128 * (ii + 1)], ident[:, :])
                    if ii == 7:
                        mm.then_inc(m_wtt, 1)
            for f in TOPS:
                f(tensor)

        @block.vector
        def _(vector):
            vector.memset(epsb[:, :], 1e-5)
            vector.memset(onescol[:, :], 1.0)
            vector.memset(intc[:, 0:1], SIGN_MASK)
            vector.memset(intc[:, 1:2], EXP_MASK)
            vector.memset(cbc[:, 0:1], EXP_BIAS_C)
            vector.memset(cbc[:, 1:2], -0.5)
            vector.memset(onesr, 1.0)
            vector.wait_ge(m_pro, 48)
            vector.tensor_tensor(sval, isv, wsv, ALU.mult)
            vector.tensor_copy(spc[:, :], epsb[:, :]).then_inc(m_prov, 1)
            vector.wait_ge(m_prot, 1)
            vector.tensor_copy(spc[:, :], epsb[:, :])
            vector.tensor_copy(spc[:, :], epsb[:, :])
            vector.tensor_copy(recip_b[:, :], P3[:, 0:1])
            vector.tensor_copy(biasb[:, 0:512], P0[:, :])
            vector.tensor_copy(biasb[:, 512:1024], P1[:, :])
            vector.tensor_copy(s_b[:, :], P2[:, 0:1])
            vector.reciprocal(recip_b[:, :], recip_b[:, :])
            vector.tensor_copy(spc[:, :], epsb[:, :])
            vector.tensor_copy(spc[:, :], epsb[:, :]).then_inc(m_prov, 1)
            for i in range(NTILE + 2):
                if i < NTILE:
                    sl = i % 2
                    vector.wait_ge(m_sc, SC0[("acc", i)])
                    vector.tensor_scalar(s0[("mean", sl)], s0[("sums", sl)], 1.0 / D, None, ALU.mult)
                    vector.tensor_scalar(s0[("var", sl)], s0[("sumsq", sl)], 1.0 / D, None, ALU.mult)
                    vector.tensor_scalar(s0[("std", sl)], s0[("mean", sl)], s0[("mean", sl)], None, ALU.mult)
                    vector.tensor_copy(spc[:, :], epsb[:, :])
                    vector.tensor_sub(s0[("var", sl)], s0[("var", sl)], s0[("std", sl)])
                    vector.tensor_copy(spc[:, :], epsb[:, :]).then_inc(m_ve, 1)
                if 1 <= i <= NTILE:
                    j = i - 1
                    sl = j % 2
                    vector.wait_ge(m_sc, SC0[("sqrt", j)])
                    vector.reciprocal(s0[("rstd", sl)], s0[("std", sl)])
                    vector.tensor_scalar(s0[("nmr", sl)], s0[("mean", sl)], -1.0, None, ALU.mult)
                    vector.tensor_copy(spc[:, :], epsb[:, :])
                    vector.tensor_tensor(s0[("nmr", sl)], s0[("nmr", sl)], s0[("rstd", sl)], ALU.mult)
                    vector.tensor_copy(spc[:, :], epsb[:, :]).then_inc(m_ve, 1)
                if i >= 2:
                    j2 = i - 2
                    vector.wait_ge(m_tt, j2 + 1)
                    vector.tensor_copy(spc[:, :], epsb[:, :])
                    vector.tensor_copy(spc[:, :], epsb[:, :])
                    vector.tensor_copy(hT[:, :, 128 * j2:128 * (j2 + 1)],
                                       P7b[:, :].rearrange("p (a b) -> p a b", a=8, b=128))
                    vector.tensor_copy(spc[:, :], epsb[:, :]).then_inc(m_ht, 1)
            KTPv = [P6b, P7b]
            def kt_copy(cc):
                vector.wait_ge(m_kt, cc + 1)
                vector.tensor_copy(spc[:, :], epsb[:, :])
                vector.tensor_copy(spc[:, :], epsb[:, :])
                vector.tensor_copy(KT_sb[cc % 2][:, :], KTPv[cc % 2][:, :])
                vector.tensor_copy(spc[:, :], epsb[:, :]).then_inc(m_ktsb, 1)
            kt_copy(0)
            for c in range(NC):
                for tc in range(TCH):
                    k = 4 * c + tc
                    vector.wait_ge(m_mm1, k + 1)
                    if c >= 2 and tc == 0:
                        vector.wait_ge(m_spill, 16 * (c - 1))
                    vector.tensor_copy(spc[:, :], epsb[:, :])
                    vector.tensor_copy(spc[:, :], epsb[:, :])
                    vector.tensor_copy(s_sb[c % 2][:, 512 * tc:512 * (tc + 1)],
                                       P[4 + k % 2][:, :])
                    vector.tensor_copy(spc[:, :], epsb[:, :]).then_inc(m_sevac, 1)
                    if tc == 1 and c + 1 < NC:
                        kt_copy(c + 1)
            vector.wait_ge(m_ones, 4 * NC)
            vector.wait_ge(m_spill, 16 * NC)
            vector.memset(onesr2, 1.0)
            vector.tensor_copy(spc[:, :], epsb[:, :])
            vector.tensor_copy(spc[:, :], epsb[:, :])
            for tc in range(TCH):
                vector.tensor_copy(nrow[:, 512 * tc:512 * (tc + 1)], P[tc][0:1, :])
            vector.tensor_copy(spc[:, :], epsb[:, :])
            vector.tensor_copy(spc[:, :], epsb[:, :]).then_inc(m_nr, 1)
            for tc in range(TCH):
                vector.wait_ge(m_bc, tc + 1)
                vector.tensor_copy(spc[:, :], epsb[:, :])
                vector.tensor_copy(spc[:, :], epsb[:, :])
                vector.tensor_copy(scB[:, 512 * tc:512 * (tc + 1)], P[tc][:, :])
                vector.tensor_copy(spc[:, :], epsb[:, :]).then_inc(m_scb, 1)
            for wr in range(8):
                vector.wait_ge(m_wtt, wr + 1)
                vector.tensor_copy(spc[:, :], epsb[:, :])
                vector.tensor_copy(spc[:, :], epsb[:, :])
                vector.tensor_copy(WT[:, :, 128 * wr:128 * (wr + 1)],
                                   P7b[:, :].rearrange("p (a b) -> p a b", a=8, b=128))
                vector.tensor_copy(spc[:, :], epsb[:, :]).then_inc(m_wev, 1)
            for f in VOPS:
                f(vector)


        @block.scalar
        def _(scalar):
            for i in range(NTILE + 2):
                if i < NTILE:
                    sl = i % 2
                    scalar.wait_ge(m_x, 16 * (i + 1))
                    if i >= 2:
                        scalar.wait_ge(m_ve, VE0[("stats", i - 2)])
                    scalar.activation(dumpB[:, :], xstage[i % 4][:, :], AF.Identity,
                                      accum_out=s0[("sums", sl)])
                    scalar.activation(dumpB[:, :], xstage[i % 4][:, :], AF.Square,
                                      accum_out=s0[("sumsq", sl)])
                    scalar.copy(spcS[:, :], epsb[:, :]).then_inc(m_sc, 1)
                if 1 <= i <= NTILE:
                    j = i - 1
                    sl = j % 2
                    scalar.wait_ge(m_ve, VE0[("stats", j)])
                    scalar.activation(s0[("std", sl)], s0[("var", sl)], AF.Sqrt,
                                      bias=epsb[:, 0:1], scale=1.0)
                    scalar.copy(spcS[:, :], epsb[:, :]).then_inc(m_sc, 1)
                if i >= 2:
                    j = i - 2
                    sl = j % 2
                    scalar.wait_ge(m_ve, VE0[("rn", j)])
                    if j >= 2:
                        scalar.wait_ge(m_tt, j - 1)
                    scalar.activation(hnorm[j % 2][:, :], xstage[j % 4][:, :], AF.Identity,
                                      bias=s0[("nmr", sl)], scale=s0[("rstd", sl)])
                    scalar.copy(spcS[:, :], epsb[:, :]).then_inc(m_sc, 1)
            for c in range(NC):
                for tc in range(TCH):
                    k = 4 * c + tc
                    scalar.wait_ge(m_sevac, k + 1)
                    if k >= 4:
                        scalar.wait_ge(m_ones, k - 3)
                    scalar.activation(sqb[k % 4][:, :], s_sb[c % 2][:, 512 * tc:512 * (tc + 1)],
                                      AF.Square).then_inc(m_sq, 1)
            scalar.wait_ge(m_nr, 1)
            scalar.activation(nrow, nrow, AF.Ln)
            scalar.activation(nrow, nrow, AF.Exp, scale=cbc[0:1, 1:2], bias=cbc[0:1, 0:1])
            scalar.copy(spcS[:, :], epsb[:, :]).then_inc(m_nl, 1)
            for f in SOPS:
                f(scalar)
            scalar.wait_ge(m_out, 16 * 8 * NG)

    _es.close()
    return nc


_NC_CACHE = [None]


def kernel(L0_attn_q, L0_attn_k, L0_attn_v, L0_ffn_weight_fp8, L0_ffn_bias,
           L0_ffn_input_scale, L0_ffn_weight_scale):
    q = np.ascontiguousarray(np.asarray(L0_attn_q, dtype=np.float32))
    k = np.ascontiguousarray(np.asarray(L0_attn_k, dtype=np.float32))
    v = np.ascontiguousarray(np.asarray(L0_attn_v, dtype=np.float32))
    w = np.ascontiguousarray(np.asarray(L0_ffn_weight_fp8, dtype=np.float32))
    b = np.ascontiguousarray(np.asarray(L0_ffn_bias, dtype=np.float32)).reshape(1, O)
    isc = np.ascontiguousarray(np.asarray(L0_ffn_input_scale, dtype=np.float32)).reshape(1, 1)
    wsc = np.ascontiguousarray(np.asarray(L0_ffn_weight_scale, dtype=np.float32)).reshape(1, 1)

    if _NC_CACHE[0] is None:
        _NC_CACHE[0] = build_program()
    nc = _NC_CACHE[0]

    in_maps = [{"x": q[core], "kk": k, "vv": v, "ww": w, "bias": b,
                "iscale": isc, "wscale": wsc} for core in range(8)]
    res = run_bass_kernel_spmd(nc, in_maps, core_ids=list(range(8)))
    out = np.stack([res.results[i]["out"] for i in range(8)], axis=0)
    return out.astype(np.float32)
